# version 04 (final)
# baseline (speedup 1.0000x reference)
import contextlib
import numpy as np
import ml_dtypes

import concourse.bass as bass
import concourse.mybir as mybir
from concourse.bass_utils import run_bass_kernel_spmd

F32 = mybir.dt.float32
BF16 = mybir.dt.bfloat16
I32 = mybir.dt.int32
U16 = mybir.dt.uint16
AF = mybir.ActivationFunctionType
ALU = mybir.AluOpType

NCORE = 8
L = 16384
D = 2048
E = 96
RDT = 64
NST = 16
DI = 5632
TOK = L // NCORE
DWIDTH = D // NCORE
CH = 1024
NCH = L // CH
EPS = 1e-5


def bfv(ap):
    return ap.bitcast(U16)[:, 1::2].bitcast(BF16)


class _Node:
    __slots__ = ("eng", "emit", "deps", "comp", "gidx", "dma", "qidx")


class Sched:

    ENGINES = ("sync", "scalar", "vector", "tensor", "gpsimd")
    RQ = 8

    def __init__(self, nc, es):
        self.nc = nc
        self.es = es
        self.csems = {e: es.enter_context(nc.semaphore(f"sc_{e}"))
                      for e in self.ENGINES}
        self.dsems = {e: [es.enter_context(nc.semaphore(f"sd_{e}{i}"))
                          for i in range(self.RQ)]
                      for e in ("sync", "gpsimd", "scalar", "vector")}
        self.ccum = {e: 0 for e in self.ENGINES}
        self.dcum = {e: [0] * self.RQ for e in self.dsems}
        self.dcount = {e: 0 for e in self.dsems}
        self.dlast = {e: [None] * self.RQ for e in self.dsems}
        self.nodes = {e: [] for e in self.ENGINES}
        self.gidx = 0
        self.writer = {}
        self.readers = {}

    def add(self, eng, emit, reads=(), writes=(), dma=0):
        deps = []
        for b in reads:
            w = self.writer.get(b)
            if w is not None:
                deps.append(w)
        for b in writes:
            w = self.writer.get(b)
            if w is not None:
                deps.append(w)
            deps.extend(self.readers.get(b, ()))
        n = _Node()
        n.eng, n.emit, n.dma = eng, emit, dma
        n.gidx = self.gidx
        self.gidx += 1
        if dma:
            q = self.dcount[eng] % self.RQ
            self.dcount[eng] += 1
            before = self.dcum[eng][q]
            self.dcum[eng][q] += 16 * dma
            n.qidx = (q, before)
            n.comp = ("d", eng, q, self.dcum[eng][q])
        else:
            n.qidx = None
            self.ccum[eng] += 1
            n.comp = ("c", eng, None, self.ccum[eng])
        n.deps = deps
        for b in reads:
            self.readers.setdefault(b, []).append(n)
        for b in writes:
            self.writer[b] = n
            self.readers[b] = []
        self.nodes[eng].append(n)
        return n

    def build(self, block):
        for ename in self.ENGINES:
            lst = self.nodes[ename]
            if not lst:
                continue
            sec = getattr(block, ename)

            def body(eng, lst=lst, ename=ename):
                for n in lst:
                    waits = {}
                    for d in n.deps:
                        kind, de, dq, dv = d.comp
                        if kind == "c":
                            key = ("c", de, None)
                        else:
                            key = ("d", de, dq)
                        if dv > waits.get(key, 0):
                            waits[key] = dv
                    if n.dma:
                        q, before = n.qidx
                        if before > 0:
                            key = ("d", ename, q)
                            if before > waits.get(key, 0):
                                waits[key] = before
                    for (kind, de, dq), v in sorted(waits.items()):
                        sem = (self.csems[de] if kind == "c"
                               else self.dsems[de][dq])
                        eng.wait_ge(sem, v)
                    insts = n.emit(eng)
                    if not isinstance(insts, (list, tuple)):
                        insts = [insts]
                    if n.dma:
                        assert len(insts) == n.dma, (len(insts), n.dma)
                        sem = self.dsems[ename][n.qidx[0]]
                        for i in insts:
                            i.then_inc(sem, 16)
                    else:
                        insts[-1].then_inc(self.csems[ename], 1)
            sec(body)


def build_xdbl():
    nc = bass.Bass("TRN2", target_bir_lowering=False, debug=False)
    XT = nc.dram_tensor("XT", [D, TOK], F32, kind="ExternalInput").ap()
    WXT = nc.dram_tensor("WXT", [D, E], F32, kind="ExternalInput").ap()
    XD = nc.dram_tensor("XD", [E, TOK], F32, kind="ExternalOutput").ap()

    es = contextlib.ExitStack()
    t = lambda nm, sh, dt: es.enter_context(nc.sbuf_tensor(nm, sh, dt))
    wx = t("wx", [128, 16 * E], F32)
    xb = [t(f"xb{i}", [128, 16 * 512], F32) for i in range(2)]
    ot = [t(f"ot{i}", [128, 512], F32) for i in range(2)]
    ps = [es.enter_context(nc.psum_tensor(f"ps{i}", [128, 512], F32))
          for i in range(2)]
    sch = Sched(nc, es)
    NT = TOK // 512

    def dma_wx(eng):
        return [eng.dma_start(out=wx[:, k * E:(k + 1) * E],
                              in_=WXT[k * 128:(k + 1) * 128, :])
                for k in range(16)]
    sch.add("sync", dma_wx, writes=("wx",), dma=16)

    for tt in range(NT):
        p = tt % 2

        def dma_x(eng, tt=tt, p=p):
            return [eng.dma_start(
                out=xb[p][:, k * 512:(k + 1) * 512],
                in_=XT[k * 128:(k + 1) * 128, tt * 512:(tt + 1) * 512])
                for k in range(16)]
        sch.add("sync", dma_x, writes=(f"xb{p}",), dma=16)

        def mm(eng, p=p):
            last = None
            for k in range(16):
                last = eng.matmul(
                    ps[p][0:E, :], wx[:, k * E:(k + 1) * E],
                    xb[p][:, k * 512:(k + 1) * 512],
                    start=(k == 0), stop=(k == 15))
            return last
        sch.add("tensor", mm, reads=("wx", f"xb{p}"), writes=(f"ps{p}",))

        def ev(eng, p=p):
            return eng.copy(out=ot[p][0:E, :], in_=ps[p][0:E, :])
        sch.add("scalar", ev, reads=(f"ps{p}",), writes=(f"ot{p}",))

        def stor(eng, tt=tt, p=p):
            return [eng.dma_start(out=XD[:, tt * 512:(tt + 1) * 512],
                                  in_=ot[p][0:E, :])]
        sch.add("gpsimd", stor, reads=(f"ot{p}",), writes=("xd_out",), dma=1)

    sch.add("gpsimd", lambda e: e.engine_nop(), reads=("xd_out",))
    with nc.Block() as block:
        sch.build(block)
    es.close()
    return nc


def build_mixer(nch=NCH, parts=('delta','mm','relu','abs','sexp','sln','spadd','dx','yi','bc','dA','dbx','scan','state','tmp','yadd','z1'), dbg=False):
    nc = bass.Bass("TRN2", target_bir_lowering=False, debug=False)
    XDF = nc.dram_tensor("XDF", [E, L], F32, kind="ExternalInput").ap()
    XTD = nc.dram_tensor("XTD", [DWIDTH, L], F32, kind="ExternalInput").ap()
    WDT = nc.dram_tensor("WDT", [RDT, DWIDTH], F32, kind="ExternalInput").ap()
    BDT = nc.dram_tensor("BDT", [DWIDTH, 1], F32, kind="ExternalInput").ap()
    ALOG = nc.dram_tensor("ALOG", [DWIDTH, NST], F32, kind="ExternalInput").ap()
    DPAR = nc.dram_tensor("DPAR", [DWIDTH, 1], F32, kind="ExternalInput").ap()
    SEL = nc.dram_tensor("SEL", [2 * NST, 2 * NST * 128], F32,
                         kind="ExternalInput").ap()
    Z1 = nc.dram_tensor("Z1", [DWIDTH, L], F32, kind="ExternalOutput").ap()
    if dbg is not False:
        DBGS = {nm: nc.dram_tensor(nm, [128, CH], F32,
                                   kind="ExternalOutput").ap()
                for nm in ("DBG_DELTA", "DBG_DX", "DBG_DA", "DBG_DBX",
                           "DBG_H", "DBG_BCB", "DBG_TMP")}

    NG = DWIDTH // 128
    NBK = CH // 512

    es = contextlib.ExitStack()
    t = lambda nm, sh, dt: es.enter_context(nc.sbuf_tensor(nm, sh, dt))
    wdt_f = t("wdt_f", [RDT, DWIDTH], F32)
    wdt_b = t("wdt_b", [RDT, DWIDTH], BF16)
    alog_t = t("alog_t", [128, NG * NST], F32)
    acoef = t("acoef", [128, NG * NST], F32)
    bdt_t = t("bdt_t", [128, NG], F32)
    d1_t = t("d1_t", [128, NG], F32)
    state = t("state", [128, NG * NST], F32)
    slab = t("slab", [E, CH], F32)
    draw_b = t("draw_b", [RDT, CH], BF16)
    xt_t = t("xt_t", [128, NG * CH], F32)
    delta = t("delta", [128, NG * CH], F32)
    dx = t("dx", [128, NG * CH], F32)
    sp_relu = [t(f"sp_relu{i}", [128, 512], F32) for i in range(2)]
    sp_min = [t(f"sp_min{i}", [128, 512], F32) for i in range(2)]
    sp_a = [t(f"sp_a{i}", [128, 512], F32) for i in range(2)]
    sp_e = [t(f"sp_e{i}", [128, 512], F32) for i in range(2)]
    sp_l = [t(f"sp_l{i}", [128, 512], F32) for i in range(2)]
    sel_t = t("sel_t", [2 * NST, 2 * NST * 128], F32)
    bc32 = t("bc32", [2 * NST, CH], F32)
    dA = [t(f"dA{i}", [128, CH], F32) for i in range(2)]
    hh = [t(f"hh{g}", [128, CH], F32) for g in range(NG)]
    dBx = [t(f"dBx{g}", [128, CH], F32) for g in range(NG)]
    tmp = [t(f"tmp{g}", [128, CH], F32) for g in range(NG)]
    yy = [[t(f"yy{g}_{i}", [128, CH], F32) for i in range(2)] for g in range(NG)]
    psd = [es.enter_context(nc.psum_tensor(f"psd{i}", [128, 512], F32))
           for i in range(2)]
    ps_bcb = es.enter_context(nc.psum_tensor("ps_bcb", [128, CH], F32))
    ps_bcc = es.enter_context(nc.psum_tensor("ps_bcc", [128, CH], F32))
    sch = Sched(nc, es)
    sch.add("sync", lambda e: [e.dma_start(out=sel_t[:], in_=SEL)],
            writes=("sel",), dma=1)

    def c_in(eng):
        out = [eng.dma_start(out=wdt_f[:], in_=WDT)]
        for g in range(NG):
            out.append(eng.dma_start(out=alog_t[:, g * NST:(g + 1) * NST],
                                     in_=ALOG[g * 128:(g + 1) * 128, :]))
            out.append(eng.dma_start(out=bdt_t[:, g:g + 1],
                                     in_=BDT[g * 128:(g + 1) * 128, :]))
            out.append(eng.dma_start(out=d1_t[:, g:g + 1],
                                     in_=DPAR[g * 128:(g + 1) * 128, :]))
        return out
    sch.add("sync", c_in, writes=("wdt_f", "alog", "bdt", "dpar"),
            dma=1 + 3 * NG)
    sch.add("scalar", lambda e: e.activation(out=acoef[:], in_=alog_t[:],
                                             func=AF.Exp),
            reads=("alog",), writes=("acoef",))
    sch.add("vector", lambda e: e.tensor_scalar_mul(acoef[:], acoef[:], -1.0),
            reads=("acoef",), writes=("acoef",))
    sch.add("vector", lambda e: e.tensor_scalar_add(d1_t[:], d1_t[:], 1.0),
            reads=("dpar",), writes=("dpar",))
    sch.add("vector", lambda e: e.memset(state[:], 0.0), writes=("state",))
    sch.add("scalar", lambda e: e.copy(out=wdt_b[:], in_=wdt_f[:]),
            reads=("wdt_f",), writes=("wdt_b",))

    for ci in range(nch):
        c0 = ci * CH

        def d_slab(eng, c0=c0):
            return [eng.dma_start(out=slab[:], in_=XDF[:, c0:c0 + CH])]
        sch.add("sync", d_slab, writes=("slab",), dma=1)

        def d_xt(eng, c0=c0):
            return [eng.dma_start(
                out=xt_t[:, g * CH:(g + 1) * CH],
                in_=XTD[g * 128:(g + 1) * 128, c0:c0 + CH])
                for g in range(NG)]
        sch.add("sync", d_xt, writes=("xt",), dma=NG)

        sch.add("scalar", lambda e: e.copy(out=draw_b[:], in_=slab[0:RDT, :]),
                reads=("slab",), writes=("draw_b",))
        def d_bc32(eng, c0=c0):
            return [eng.dma_start(out=bc32[:], in_=XDF[RDT:E, c0:c0 + CH])]
        sch.add("sync", d_bc32, writes=("bc32",), dma=1)

        for g in range(NG):
            for blk in range(NBK if 'delta' in parts else 0):
                p = (g * NBK + blk) % 2
                sl = slice(blk * 512, (blk + 1) * 512)

                def mm(eng, g=g, sl=sl, p=p):
                    return eng.matmul(psd[p][:],
                                      wdt_b[:, g * 128:(g + 1) * 128],
                                      draw_b[:, sl], start=True, stop=True)
                if 'mm' in parts:
                    sch.add("tensor", mm, reads=("draw_b", "wdt_b"),
                            writes=(f"psd{p}",))

                def v_relu(eng, g=g, p=p):
                    return eng.tensor_scalar(out=sp_relu[p][:], in0=psd[p][:],
                                             scalar1=bdt_t[:, g:g + 1],
                                             scalar2=0.0,
                                             op0=ALU.add, op1=ALU.max)
                if 'relu' in parts:
                    sch.add("vector", v_relu, reads=(f"psd{p}", "bdt"),
                            writes=(f"sprelu{p}",))

                def v_min(eng, g=g, p=p):
                    return eng.tensor_scalar(out=sp_min[p][:], in0=psd[p][:],
                                             scalar1=bdt_t[:, g:g + 1],
                                             scalar2=0.0,
                                             op0=ALU.add, op1=ALU.min)

                def v_negabs(eng, p=p):
                    return eng.tensor_sub(sp_a[p][:], sp_min[p][:],
                                          sp_relu[p][:])
                if 'abs' in parts:
                    sch.add("vector", v_min, reads=(f"psd{p}", "bdt"),
                            writes=(f"spmin{p}",))
                    sch.add("vector", v_negabs,
                            reads=(f"spmin{p}", f"sprelu{p}"),
                            writes=(f"spra{p}",))

                def a_exp(eng, p=p):
                    return eng.activation(out=sp_e[p][:], in_=sp_a[p][:],
                                          func=AF.Exp)
                if 'sexp' in parts:
                    sch.add("scalar", a_exp, reads=(f"spra{p}",),
                            writes=(f"spe{p}",))

                def a_ln(eng, p=p):
                    return eng.activation(out=sp_l[p][:], in_=sp_e[p][:],
                                          func=AF.Ln, bias=1.0)
                if 'sln' in parts:
                    sch.add("scalar", a_ln, reads=(f"spe{p}",),
                            writes=(f"spl{p}",))

                def spadd(eng, g=g, sl=sl, p=p):
                    return eng.tensor_add(
                        delta[:, g * CH + sl.start:g * CH + sl.stop],
                        sp_relu[p][:], sp_l[p][:])
                if 'spadd' in parts:
                    sch.add("vector", spadd, reads=(f"sprelu{p}", f"spl{p}"),
                            writes=(f"delta{g}",))

            if 'dx' in parts:
                def v_dx(eng, g=g):
                    gs = slice(g * CH, (g + 1) * CH)
                    return eng.tensor_mul(dx[:, gs], delta[:, gs], xt_t[:, gs])
                sch.add("vector", v_dx, reads=(f"delta{g}", "xt"),
                        writes=(f"dx{g}",))

            if 'yi' in parts:
                def v_yi(eng, g=g):
                    gs = slice(g * CH, (g + 1) * CH)
                    return eng.tensor_scalar_mul(yy[g][0][:], xt_t[:, gs],
                                                 d1_t[:, g:g + 1])
                sch.add("vector", v_yi, reads=("xt", "dpar"),
                        writes=(f"yy{g}_0",))

        if dbg is not False and ci == (0 if dbg is True else dbg):
            def d_dbge(eng):
                return [eng.dma_start(out=DBGS["DBG_DX"], in_=delta[:, 0:CH])]
            sch.add("sync", d_dbge, reads=("delta0",), writes=("dbge",),
                    dma=1)
        for n in range(NST if 'bc' in parts else 0):
            p = n % 2

            def t_bc(eng, n=n):
                for blk in range(CH // 512):
                    sl = slice(blk * 512, (blk + 1) * 512)
                    eng.matmul(ps_bcb[:, sl],
                               sel_t[:, n * 128:(n + 1) * 128],
                               bc32[:, sl], start=True, stop=True)
                last = None
                for blk in range(CH // 512):
                    sl = slice(blk * 512, (blk + 1) * 512)
                    last = eng.matmul(ps_bcc[:, sl],
                                      sel_t[:, (NST + n) * 128:
                                            (NST + n + 1) * 128],
                                      bc32[:, sl], start=True, stop=True)
                return last
            sch.add("tensor", t_bc, reads=("sel", "bc32"),
                    writes=("bcb", "bcc"))

            for g in range(NG):
                pa = (n * NG + g) % 2
                if 'dA' not in parts:
                    continue

                def a_dA(eng, g=g, n=n, pa=pa):
                    gs = slice(g * CH, (g + 1) * CH)
                    return eng.activation(
                        out=dA[pa][:], in_=delta[:, gs], func=AF.Exp,
                        scale=acoef[:, g * NST + n:g * NST + n + 1])
                sch.add("scalar", a_dA, reads=(f"delta{g}", "acoef"),
                        writes=(f"dA{pa}",))

                if 'dbx' in parts:
                    def v_dbx(eng, g=g):
                        gs = slice(g * CH, (g + 1) * CH)
                        return eng.tensor_mul(dBx[g][:], dx[:, gs], ps_bcb[:])
                    sch.add("vector", v_dbx, reads=(f"dx{g}", "bcb"),
                            writes=(f"dBx{g}",))

                if 'scan' in parts:
                    def v_scan(eng, g=g, n=n, pa=pa):
                        return eng.tensor_tensor_scan(
                            out=hh[g][:], data0=dA[pa][:], data1=dBx[g][:],
                            initial=state[:, g * NST + n:g * NST + n + 1],
                            op0=ALU.mult, op1=ALU.add)
                    sch.add("vector", v_scan,
                            reads=(f"dA{pa}", f"dBx{g}", "state"),
                            writes=(f"hh{g}",))

                if 'state' in parts:
                    def v_state(eng, g=g, n=n):
                        return eng.tensor_copy(
                            out=state[:, g * NST + n:g * NST + n + 1],
                            in_=hh[g][:, CH - 1:CH])
                    sch.add("vector", v_state, reads=(f"hh{g}",),
                            writes=("state",))

                if 'tmp' in parts:
                    def v_tmp(eng, g=g):
                        return eng.tensor_mul(tmp[g][:], hh[g][:], ps_bcc[:])
                    sch.add("vector", v_tmp, reads=(f"hh{g}", "bcc"),
                            writes=(f"tmp{g}",))

                yp = n % 2
                if 'yadd' in parts:
                    def v_yadd(eng, g=g, yp=yp):
                        return eng.tensor_add(yy[g][1 - yp][:], yy[g][yp][:],
                                              tmp[g][:])
                    sch.add("vector", v_yadd,
                            reads=(f"tmp{g}", f"yy{g}_{yp}"),
                            writes=(f"yy{g}_{1 - yp}",))

        if dbg is not False and ci == (0 if dbg is True else dbg):
            def d_dbg(eng):
                return [
                    eng.dma_start(out=DBGS["DBG_DELTA"], in_=delta[:, 0:CH]),
                    eng.dma_start(out=DBGS["DBG_DA"], in_=dA[1][:]),
                    eng.dma_start(out=DBGS["DBG_DBX"], in_=dBx[1][:]),
                    eng.dma_start(out=DBGS["DBG_H"], in_=hh[1][:]),
                    eng.dma_start(out=DBGS["DBG_TMP"], in_=tmp[1][:]),
                ]
            sch.add("sync", d_dbg,
                    reads=("delta0", "dx0", "dA1", "dBx1", "hh1",
                           "tmp1"),
                    writes=("dbg",), dma=5)
        for g in range(NG):
            def d_z1(eng, c0=c0, g=g):
                return [eng.dma_start(out=Z1[g * 128:(g + 1) * 128, c0:c0 + CH],
                                      in_=yy[g][0][:])]
            sch.add("sync", d_z1, reads=(f"yy{g}_0",), writes=("z1_out",),
                    dma=1)

    sch.add("gpsimd", lambda e: e.engine_nop(), reads=("z1_out",))
    with nc.Block() as block:
        sch.build(block)
    es.close()
    return nc


def build_ffn(phases=(1, 1, 1, 1)):
    nc = bass.Bass("TRN2", target_bir_lowering=False, debug=False)
    Z1T = nc.dram_tensor("Z1T", [D, TOK], F32, kind="ExternalInput").ap()
    GWT = nc.dram_tensor("GWT", [D, DI], F32, kind="ExternalInput").ap()
    UWT = nc.dram_tensor("UWT", [D, DI], F32, kind="ExternalInput").ap()
    DWT = nc.dram_tensor("DWT", [DI, D], F32, kind="ExternalInput").ap()
    GS = nc.dram_tensor("GS", [DI, 1], F32, kind="ExternalInput").ap()
    US = nc.dram_tensor("US", [DI, 1], F32, kind="ExternalInput").ap()
    DS = nc.dram_tensor("DS", [D, 1], F32, kind="ExternalInput").ap()
    ONE1 = nc.dram_tensor("ONE1", [128, 1], F32, kind="ExternalInput").ap()
    ONER = nc.dram_tensor("ONER", [1, 128], F32, kind="ExternalInput").ap()
    IDB = nc.dram_tensor("IDB", [128, 128], BF16, kind="ExternalInput").ap()
    OUT = nc.dram_tensor("OUT", [TOK, D], F32, kind="ExternalOutput").ap()
    HDN = nc.dram_tensor("HDN", [DI, TOK], BF16)
    Z2S = nc.dram_tensor("Z2S", [D, TOK], BF16)

    NIC = DI // 128
    NHC = D // 128
    NTT = TOK // 512

    es = contextlib.ExitStack()
    t = lambda nm, sh, dt: es.enter_context(nc.sbuf_tensor(nm, sh, dt))
    gs_t = t("gs_t", [128, NIC], F32)
    us_t = t("us_t", [128, NIC], F32)
    ds_t = t("ds_t", [128, NHC], F32)
    one1 = t("one1", [128, 1], F32)
    oner = t("oner", [1, 128], F32)
    idb = t("idb", [128, 128], BF16)
    z1 = [t(f"z1_{i}", [128, 512], F32) for i in range(2)]
    z1b = t("z1b", [128, 512], BF16)
    sqb = t("sqb", [128, 512], BF16)
    mrow = t("mrow", [1, 512], F32)
    vrow = t("vrow", [1, 512], F32)
    msq = t("msq", [1, 512], F32)
    lrow = t("lrow", [1, 512], F32)
    rrow = t("rrow", [1, 512], F32)
    ab = [t(f"ab_{k}", [128, NTT * 512], BF16) for k in range(NHC)]
    asc = t("asc", [128, 512], F32)
    wga = [t(f"wga{i}", [128, 16 * 128], F32) for i in range(2)]
    wua = [t(f"wua{i}", [128, 16 * 128], F32) for i in range(2)]
    sg = t("sg", [128, 512], BF16)
    su = t("su", [128, 512], BF16)
    hdnt = [t(f"hdnt{i}", [128, 512], BF16) for i in range(2)]
    dwa = [t(f"dwa{i}", [128, 128], F32) for i in range(2)]
    hsb = [t(f"hsb_{i}", [128, 512], BF16) for i in range(NIC)]
    z2t = [t(f"z2t{i}", [128, 512], BF16) for i in range(2)]
    z2l = [t(f"z2l_{k}", [128, 128], BF16) for k in range(NHC)]
    sq2 = t("sq2", [128, 2048], BF16)
    scol = t("scol", [128, 1], F32)
    sscol = t("sscol", [128, 1], F32)
    mcol = t("mcol", [128, 1], F32)
    vcol = t("vcol", [128, 1], F32)
    lcol = t("lcol", [128, 1], F32)
    rcol = t("rcol", [128, 1], F32)
    outt = t("outt", [128, 2048], F32)
    epsr = t("epsr", [1, 1], F32)
    epsc = t("epsc", [128, 1], F32)

    pp = [es.enter_context(nc.psum_tensor(f"pp{i}", [128, 512], F32))
          for i in range(4)]
    ps_t = es.enter_context(nc.psum_tensor("ps_t", [128, 2048], BF16))

    sch = Sched(nc, es)

    def c_in(eng):
        out = []
        for k in range(NIC):
            out.append(eng.dma_start(out=gs_t[:, k:k + 1],
                                     in_=GS[k * 128:(k + 1) * 128, :]))
            out.append(eng.dma_start(out=us_t[:, k:k + 1],
                                     in_=US[k * 128:(k + 1) * 128, :]))
        for k in range(NHC):
            out.append(eng.dma_start(out=ds_t[:, k:k + 1],
                                     in_=DS[k * 128:(k + 1) * 128, :]))
        out.append(eng.dma_start(out=one1[:], in_=ONE1))
        out.append(eng.dma_start(out=oner[:], in_=ONER))
        out.append(eng.dma_start(out=idb[:], in_=IDB))
        return out
    sch.add("sync", c_in, writes=("consts",), dma=2 * NIC + NHC + 3)

    def c_eps(eng):
        eng.memset(epsr[:], EPS)
        return eng.memset(epsc[:], EPS)
    sch.add("vector", c_eps, writes=("eps",))

    for tt in range(NTT if phases[0] else 0):
        t0 = tt * 512
        for k in range(NHC):
            p = k % 2

            def d_z1(eng, t0=t0, k=k, p=p):
                return [eng.dma_start(
                    out=z1[p][:],
                    in_=Z1T[k * 128:(k + 1) * 128, t0:t0 + 512])]
            sch.add("sync", d_z1, writes=(f"z1{p}",), dma=1)

            def cast(eng, p=p):
                return eng.tensor_copy(out=z1b[:], in_=z1[p][:])
            sch.add("vector", cast, reads=(f"z1{p}",), writes=("z1b",))

            def st1(eng, k=k):
                return eng.matmul(pp[0][0:1, :], bfv(one1[:]), z1b[:],
                                  start=(k == 0), stop=(k == NHC - 1))
            sch.add("tensor", st1, reads=("z1b", "consts"), writes=("pp0",))

            def sq(eng):
                return eng.activation(out=sqb[:], in_=z1b[:], func=AF.Square)
            sch.add("scalar", sq, reads=("z1b",), writes=("sqb",))

            def st2(eng, k=k):
                return eng.matmul(pp[3][0:1, :], bfv(one1[:]), sqb[:],
                                  start=(k == 0), stop=(k == NHC - 1))
            sch.add("tensor", st2, reads=("sqb", "consts"), writes=("pp3",))

        sch.add("vector", lambda e: e.tensor_scalar_mul(mrow[:],
                pp[0][0:1, :], 1.0 / D), reads=("pp0",), writes=("mrow",))
        sch.add("vector", lambda e: e.tensor_scalar_mul(vrow[:],
                pp[3][0:1, :], 1.0 / D), reads=("pp3",), writes=("vrow0",))
        sch.add("vector", lambda e: e.tensor_mul(msq[:], mrow[:], mrow[:]),
                reads=("mrow",), writes=("msq",))
        sch.add("vector", lambda e: e.tensor_sub(vrow[:], vrow[:], msq[:]),
                reads=("vrow0", "msq", "mrow"), writes=("mv",))

        def rstd_a(eng):
            return eng.activation(out=lrow[:], in_=vrow[:], func=AF.Sqrt,
                                  bias=epsr[0:1, 0:1])
        sch.add("scalar", rstd_a, reads=("mv", "eps"), writes=("srow",))

        def rstd_b(eng):
            return eng.reciprocal(rrow[:], lrow[:])
        sch.add("vector", rstd_b, reads=("srow",), writes=("rstd",))

        def bc(eng):
            eng.matmul(pp[1][:], oner[:], mrow[:], start=True, stop=True)
            return eng.matmul(pp[2][:], oner[:], rrow[:], start=True, stop=True)
        sch.add("tensor", bc, reads=("mrow", "rstd", "consts"),
                writes=("pp1", "pp2"))

        for k in range(NHC):
            p = k % 2

            def d_z1b(eng, t0=t0, k=k, p=p):
                return [eng.dma_start(
                    out=z1[p][:],
                    in_=Z1T[k * 128:(k + 1) * 128, t0:t0 + 512])]
            sch.add("sync", d_z1b, writes=(f"z1{p}",), dma=1)

            def an1(eng, p=p):
                return eng.tensor_sub(asc[:], z1[p][:], pp[1][:])
            sch.add("vector", an1, reads=(f"z1{p}", "pp1"), writes=("asc",))

            def an2(eng, k=k, tt=tt):
                return eng.tensor_mul(ab[k][:, tt * 512:(tt + 1) * 512],
                                      asc[:], pp[2][:])
            sch.add("vector", an2, reads=("asc", "pp2"), writes=(f"a{k}",))

    for ic in range(NIC if phases[1] else 0):
        p = ic % 2

        def d_w(eng, ic=ic, p=p):
            out = []
            for k in range(NHC):
                out.append(eng.dma_start(
                    out=wga[p][:, k * 128:(k + 1) * 128],
                    in_=GWT[k * 128:(k + 1) * 128, ic * 128:(ic + 1) * 128]))
                out.append(eng.dma_start(
                    out=wua[p][:, k * 128:(k + 1) * 128],
                    in_=UWT[k * 128:(k + 1) * 128, ic * 128:(ic + 1) * 128]))
            return out
        sch.add("gpsimd", d_w, writes=(f"wg{p}",), dma=2 * NHC)

        for tt in range(NTT):
            pg = tt % 2
            ph = (ic * NTT + tt) % 2

            def mmgu(eng, tt=tt, p=p, pg=pg):
                ts = slice(tt * 512, (tt + 1) * 512)
                for k in range(NHC):
                    eng.matmul(pp[pg][:],
                               bfv(wga[p][:, k * 128:(k + 1) * 128]),
                               ab[k][:, ts], start=(k == 0),
                               stop=(k == NHC - 1))
                last = None
                for k in range(NHC):
                    last = eng.matmul(
                        pp[2 + pg][:],
                        bfv(wua[p][:, k * 128:(k + 1) * 128]),
                        ab[k][:, ts], start=(k == 0), stop=(k == NHC - 1))
                return last
            sch.add("tensor", mmgu,
                    reads=tuple([f"wg{p}"] + [f"a{k}" for k in range(NHC)]),
                    writes=(f"pp{pg}", f"pp{2 + pg}"))

            def silu(eng, ic=ic, pg=pg):
                return eng.activation(out=sg[:], in_=pp[pg][:], func=AF.Silu,
                                      scale=gs_t[:, ic:ic + 1])
            sch.add("scalar", silu, reads=(f"pp{pg}", "consts"),
                    writes=("sg",))

            def uev1(eng, ic=ic, pg=pg):
                return eng.tensor_scalar_mul(su[:], pp[2 + pg][:],
                                             us_t[:, ic:ic + 1])
            sch.add("vector", uev1, reads=(f"pp{2 + pg}", "consts"),
                    writes=("su",))

            def uev2(eng, ph=ph):
                return eng.tensor_mul(hdnt[ph][:], sg[:], su[:])
            sch.add("vector", uev2, reads=("sg", "su"), writes=(f"hdnt{ph}",))

            def d_h(eng, ic=ic, tt=tt, ph=ph):
                return [eng.dma_start(
                    out=HDN.ap()[ic * 128:(ic + 1) * 128,
                                 tt * 512:(tt + 1) * 512],
                    in_=hdnt[ph][:])]
            sch.add("sync", d_h, reads=(f"hdnt{ph}",), writes=("hdn_spill",),
                    dma=1)

    for tt in range(NTT if phases[2] else 0):
        def d_hall(eng, tt=tt):
            return [eng.dma_start(
                out=hsb[i][:],
                in_=HDN.ap()[i * 128:(i + 1) * 128, tt * 512:(tt + 1) * 512])
                for i in range(NIC)]
        sch.add("sync", d_hall, reads=("hdn_spill",), writes=("hsb",), dma=NIC)

        for hc in range(NHC):
            pd = hc % 2
            for ic in range(NIC):
                p = ic % 2

                def d_dw(eng, hc=hc, ic=ic, p=p):
                    return [eng.dma_start(
                        out=dwa[p][:],
                        in_=DWT[ic * 128:(ic + 1) * 128,
                                hc * 128:(hc + 1) * 128])]
                sch.add("gpsimd", d_dw, writes=(f"dw{p}",), dma=1)

                def mmd(eng, ic=ic, p=p, pd=pd):
                    return eng.matmul(pp[pd][:], bfv(dwa[p][:]), hsb[ic][:],
                                      start=(ic == 0), stop=(ic == NIC - 1))
                sch.add("tensor", mmd, reads=(f"dw{p}", "hsb"),
                        writes=(f"pp{pd}",))

            def z2f(eng, hc=hc, tt=tt, pd=pd):
                return eng.scalar_tensor_tensor(
                    out=z2t[pd][:], in0=pp[pd][:], scalar=ds_t[:, hc:hc + 1],
                    in1=ab[hc][:, tt * 512:(tt + 1) * 512],
                    op0=ALU.mult, op1=ALU.add)
            sch.add("vector", z2f, reads=(f"pp{pd}", f"a{hc}", "consts"),
                    writes=(f"z2t{pd}",))

            def d_z2(eng, hc=hc, tt=tt, pd=pd):
                return [eng.dma_start(
                    out=Z2S.ap()[hc * 128:(hc + 1) * 128,
                                 tt * 512:(tt + 1) * 512],
                    in_=z2t[pd][:])]
            sch.add("sync", d_z2, reads=(f"z2t{pd}",), writes=("z2_spill",),
                    dma=1)

    NTB = TOK // 128
    for tb in range(NTB if phases[3] else 0):
        c0 = tb * 128

        def d_z2l(eng, c0=c0):
            return [eng.dma_start(
                out=z2l[k][:],
                in_=Z2S.ap()[k * 128:(k + 1) * 128, c0:c0 + 128])
                for k in range(NHC)]
        sch.add("sync", d_z2l, reads=("z2_spill",), writes=("z2l",), dma=NHC)

        def tr(eng):
            last = None
            for k in range(NHC):
                last = eng.transpose(ps_t[:, k * 128:(k + 1) * 128],
                                     z2l[k][:], idb[:])
            return last
        sch.add("tensor", tr, reads=("z2l", "consts"), writes=("ps_t",))

        def st(eng):
            return eng.reduce_sum(scol[:], ps_t[:], axis=mybir.AxisListType.X)
        sch.add("vector", st, reads=("ps_t",), writes=("scol",))

        def sqa(eng):
            return eng.activation(out=sq2[:], in_=ps_t[:], func=AF.Square)
        sch.add("scalar", sqa, reads=("ps_t", "scol"), writes=("sq2",))

        def st2b(eng):
            return eng.reduce_sum(sscol[:], sq2[:], axis=mybir.AxisListType.X)
        sch.add("vector", st2b, reads=("sq2",), writes=("sscol",))

        sch.add("vector", lambda e: e.tensor_scalar_mul(mcol[:], scol[:],
                1.0 / D), reads=("scol",), writes=("mcol",))
        sch.add("vector", lambda e: e.tensor_scalar_mul(vcol[:], sscol[:],
                1.0 / D), reads=("sscol",), writes=("vcol0",))
        sch.add("vector", lambda e: e.tensor_mul(lcol[:], mcol[:], mcol[:]),
                reads=("mcol",), writes=("lcol0",))
        sch.add("vector", lambda e: e.tensor_sub(vcol[:], vcol[:], lcol[:]),
                reads=("vcol0", "lcol0", "mcol"), writes=("mv2",))

        def rstd2_a(eng):
            return eng.activation(out=lcol[:], in_=vcol[:], func=AF.Sqrt,
                                  bias=epsc[:, 0:1])
        sch.add("scalar", rstd2_a, reads=("mv2", "eps", "lcol0"),
                writes=("lcol0",))

        def rstd2_b(eng):
            return eng.reciprocal(rcol[:], lcol[:])
        sch.add("vector", rstd2_b, reads=("lcol0",), writes=("rstd2",))

        def norm(eng):
            return eng.tensor_scalar(out=outt[:], in0=ps_t[:],
                                     scalar1=mcol[:, 0:1],
                                     scalar2=rcol[:, 0:1],
                                     op0=ALU.subtract, op1=ALU.mult)
        sch.add("vector", norm, reads=("ps_t", "mv2", "mcol", "rstd2"),
                writes=("outt",))

        def d_out(eng, c0=c0):
            return [eng.dma_start(out=OUT[c0:c0 + 128, :], in_=outt[:])]
        sch.add("sync", d_out, reads=("outt",), writes=("out",), dma=1)

    if phases[3]:
        sch.add("gpsimd", lambda e: e.engine_nop(), reads=("out",))
    with nc.Block() as block:
        sch.build(block)
    es.close()
    return nc


def kernel(L0_attn_x, L0_attn_W_xproj, L0_attn_W_dt, L0_attn_b_dt,
           L0_attn_A_log, L0_attn_D_param, L0_ffn_gate_w, L0_ffn_up_w,
           L0_ffn_down_w, L0_ffn_gate_s, L0_ffn_up_s, L0_ffn_down_s):
    cc = np.ascontiguousarray
    xT = cc(np.asarray(L0_attn_x)[0].T)
    WXT = cc(np.asarray(L0_attn_W_xproj).T)
    WDTT = cc(np.asarray(L0_attn_W_dt).T)
    bdt = cc(np.asarray(L0_attn_b_dt).reshape(D, 1))
    alog = cc(np.asarray(L0_attn_A_log))
    dpar = cc(np.asarray(L0_attn_D_param).reshape(D, 1))
    GWT = cc(np.asarray(L0_ffn_gate_w).T)
    UWT = cc(np.asarray(L0_ffn_up_w).T)
    DWT = cc(np.asarray(L0_ffn_down_w).T)
    gs = cc(np.asarray(L0_ffn_gate_s).reshape(DI, 1))
    us = cc(np.asarray(L0_ffn_up_s).reshape(DI, 1))
    ds = cc(np.asarray(L0_ffn_down_s).reshape(D, 1))

    nc1 = build_xdbl()
    maps1 = [{"XT": cc(xT[:, c * TOK:(c + 1) * TOK]), "WXT": WXT}
             for c in range(NCORE)]
    r1 = run_bass_kernel_spmd(nc1, maps1, core_ids=list(range(NCORE)))
    xd_full = cc(np.concatenate([r1.results[c]["XD"] for c in range(NCORE)],
                                axis=1))

    nc2 = build_mixer()
    sel = np.zeros((2 * NST, 2 * NST * 128), np.float32)
    for n in range(2 * NST):
        sel[n, n * 128:(n + 1) * 128] = 1.0
    maps2 = []
    for c in range(NCORE):
        sl = slice(c * DWIDTH, (c + 1) * DWIDTH)
        maps2.append({
            "XDF": xd_full,
            "SEL": sel,
            "XTD": cc(xT[sl, :]),
            "WDT": cc(WDTT[:, sl]),
            "BDT": cc(bdt[sl]),
            "ALOG": cc(alog[sl]),
            "DPAR": cc(dpar[sl]),
        })
    r2 = run_bass_kernel_spmd(nc2, maps2, core_ids=list(range(NCORE)))
    z1_full = cc(np.concatenate([r2.results[c]["Z1"] for c in range(NCORE)],
                                axis=0))

    nc3 = build_ffn()
    one1 = np.ones((128, 1), np.float32)
    oner = np.ones((1, 128), np.float32)
    idb = np.eye(128, dtype=ml_dtypes.bfloat16)
    maps3 = []
    for c in range(NCORE):
        ts = slice(c * TOK, (c + 1) * TOK)
        maps3.append({
            "Z1T": cc(z1_full[:, ts]),
            "GWT": GWT, "UWT": UWT, "DWT": DWT,
            "GS": gs, "US": us, "DS": ds,
            "ONE1": one1, "ONER": oner, "IDB": idb,
        })
    r3 = run_bass_kernel_spmd(nc3, maps3, core_ids=list(range(NCORE)))
    out = np.concatenate([r3.results[c]["OUT"] for c in range(NCORE)], axis=0)
    return out.reshape(1, L, D).astype(np.float32)
